# version 01
# baseline (speedup 1.0000x reference)
import numpy as np
import concourse.bass as bass
import concourse.mybir as mybir
from concourse.bass_utils import run_bass_kernel_spmd

F32 = mybir.dt.float32
BF16 = mybir.dt.bfloat16
AF = mybir.ActivationFunctionType
ALU = mybir.AluOpType
AX = mybir.AxisListType

NCORE = 8
BPC = 2
T = 2048
DM = 1024
DI = 2048
DS = 16
TOK = BPC * T
EPS = 1e-5


class Chain:

    def __init__(self, nc, name):
        self.h = nc.alloc_semaphore(name)
        self.n = 0

    def push(self, inst, k=1):
        inst.then_inc(self.h, k)
        self.n += k
        return self.n

    def wait(self, eng, tok=None):
        eng.wait_ge(self.h, self.n if tok is None else tok)


def _ln_emit(nc, bk, ch, xtile, xn_tm, lnscr, lnm, lns, lnr, eps_col=None):

    def vpart1(v):
        ch.wait(v)
        t1 = ch.push(v.tensor_reduce(lnm[:, :], xtile[:, :], AX.X, ALU.add))
        ch.wait(v, t1)
        t2 = ch.push(v.tensor_scalar(lnm[:, :], lnm[:, :], 1.0 / DM, None, ALU.mult))
        ch.wait(v, t2)
        ch.push(v.tensor_scalar(xtile[:, :], xtile[:, :], lnm[:, :], None, ALU.subtract))
    bk.vector(vpart1)

    def spart(s):
        ch.wait(s)
        t1 = ch.push(s.activation(lnscr[:, :], xtile[:, :], AF.Square, accum_out=lns[:, :]))
        ch.wait(s, t1)
        ch.push(s.activation(lnr[:, :], lns[:, :], AF.Sqrt, scale=1.0 / DM, bias=eps_col[:, :]))
    bk.scalar(spart)

    def vpart2(v):
        ch.wait(v)
        t1 = ch.push(v.reciprocal(lnr[:, :], lnr[:, :]))
        ch.wait(v, t1)
        ch.push(v.tensor_scalar(xn_tm[:, :], xtile[:, :], lnr[:, :], None, ALU.mult))
    bk.vector(vpart2)


def build_moe(dbg=False):
    nc = bass.Bass("TRN2", target_bir_lowering=False, debug=False)
    x_in = nc.dram_tensor("x_in", [TOK, DM], F32, kind="ExternalInput").ap()
    wr_d = nc.dram_tensor("wr", [DM, 8], F32, kind="ExternalInput").ap()
    wg_d = nc.dram_tensor("wg", [8, DM, DI], F32, kind="ExternalInput").ap()
    bg_d = nc.dram_tensor("bg", [128, 16, 8], F32, kind="ExternalInput").ap()
    wu_d = nc.dram_tensor("wu", [8, DM, DI], F32, kind="ExternalInput").ap()
    bu_d = nc.dram_tensor("bu", [128, 16, 8], F32, kind="ExternalInput").ap()
    wo_d = nc.dram_tensor("wo", [8, DI, DM], F32, kind="ExternalInput").ap()
    bo_d = nc.dram_tensor("bo", [8, DM], F32, kind="ExternalInput").ap()
    id_d = nc.dram_tensor("ident", [128, 128], F32, kind="ExternalInput").ap()
    x_out = nc.dram_tensor("x_out", [TOK, DM], F32, kind="ExternalOutput").ap()
    if dbg:
        d_xq = nc.dram_tensor("d_xq", [128, 8, 1024], F32, kind="ExternalOutput").ap()
        d_w = nc.dram_tensor("d_w", [128, 8, 8], F32, kind="ExternalOutput").ap()
        d_acc0 = nc.dram_tensor("d_acc0", [128, 8, DM], F32, kind="ExternalOutput").ap()
        d_acc = nc.dram_tensor("d_acc", [128, 8, DM], F32, kind="ExternalOutput").ap()
        d_hw = nc.dram_tensor("d_hw", [128, 16, 512], F32, kind="ExternalOutput").ap()
        d_acc1 = nc.dram_tensor("d_acc1", [128, 8, DM], F32, kind="ExternalOutput").ap()

    QT = 1024
    NQ = TOK // QT
    NTT = QT // 128

    wr_sb = nc.alloc_sbuf_tensor("wr_sb", [128, 8, 8], BF16)
    bg_sb = nc.alloc_sbuf_tensor("bg_sb", [128, 16, 8], F32)
    bu_sb = nc.alloc_sbuf_tensor("bu_sb", [128, 16, 8], F32)
    bo_sb = nc.alloc_sbuf_tensor("bo_sb", [8, DM], BF16)
    id_bf = nc.alloc_sbuf_tensor("id_bf", [128, 128], BF16)
    xq_fm = nc.alloc_sbuf_tensor("xq_fm", [128, 8, QT], BF16)
    w_tm = nc.alloc_sbuf_tensor("w_tm", [128, NTT, 8], F32)
    w_fmT = nc.alloc_sbuf_tensor("w_fmT", [8, QT], BF16)
    acc = nc.alloc_sbuf_tensor("acc", [128, NTT, DM], F32)
    wg_sb = nc.alloc_sbuf_tensor("wg_sb", [128, 8, DI], BF16)
    wu_sb = nc.alloc_sbuf_tensor("wu_sb", [128, 8, DI], BF16)
    wo_sb = nc.alloc_sbuf_tensor("wo_sb", [128, 16, DM], BF16)
    hw_sb = nc.alloc_sbuf_tensor("hw_sb", [128, 16, 512], BF16)
    gs_sb = nc.alloc_sbuf_tensor("gs_sb", [128, 512], BF16)
    xtile = nc.alloc_sbuf_tensor("xtile", [128, DM], F32)
    xn_tm = nc.alloc_sbuf_tensor("xn_tm", [128, DM], BF16)
    lnscr = nc.alloc_sbuf_tensor("lnscr", [128, DM], BF16)
    lnm = nc.alloc_sbuf_tensor("lnm", [128, 1], F32)
    lns = nc.alloc_sbuf_tensor("lns", [128, 1], F32)
    lnr = nc.alloc_sbuf_tensor("lnr", [128, 1], F32)
    r_pr = nc.alloc_sbuf_tensor("r_pr", [128, 8], F32)
    r_m1 = nc.alloc_sbuf_tensor("r_m1", [128, 1], F32)
    r_m2 = nc.alloc_sbuf_tensor("r_m2", [128, 1], F32)
    r_k1 = nc.alloc_sbuf_tensor("r_k1", [128, 8], F32)
    r_k2 = nc.alloc_sbuf_tensor("r_k2", [128, 8], F32)
    r_t = nc.alloc_sbuf_tensor("r_t", [128, 8], F32)
    r_den = nc.alloc_sbuf_tensor("r_den", [128, 1], F32)
    w_bf = nc.alloc_sbuf_tensor("w_bf", [128, 8], BF16)
    eps_col = nc.alloc_sbuf_tensor("eps_col", [128, 1], F32)
    o_tm = nc.alloc_sbuf_tensor("o_tm", [128, DM], F32)

    ps_tr = [nc.alloc_psum_tensor(f"ps_tr{i}", [128, 1024], BF16) for i in range(2)]
    ps_r = nc.alloc_psum_tensor("ps_r", [128, 8], F32)
    ps_g = nc.alloc_psum_tensor("ps_g", [128, 512], F32)
    ps_u = nc.alloc_psum_tensor("ps_u", [128, 512], F32)
    ps_o = [nc.alloc_psum_tensor("ps_o0", [128, DM], F32)]
    ps_o.append(ps_o[0])

    ch = Chain(nc, "ch")
    pe_s = Chain(nc, "pe_s")
    ev_s = Chain(nc, "ev_s")
    wdma = Chain(nc, "wdma")
    outs = Chain(nc, "outs")

    with nc.Block() as bk:
        def gsec(g):
            ch.push(g.dma_start(wr_sb[:, :, :], wr_d.rearrange("(k p) e -> p k e", p=128)), 16)
            ch.push(g.dma_start(bg_sb[:, :, :], bg_d), 16)
            ch.push(g.dma_start(bu_sb[:, :, :], bu_d), 16)
            ch.push(g.dma_start(bo_sb[:, :], bo_d), 16)
            ch.push(g.dma_start(id_bf[:, :], id_d), 16)
            g.memset(eps_col[:, :], EPS)
        bk.gpsimd(gsec)

    for q in range(NQ):
        with nc.Block() as bk:
            for it in range(NTT):
                tok0 = q * QT + it * 128

                def dmain(sy, tok0=tok0):
                    ch.wait(sy)
                    ch.push(sy.dma_start(xtile[:, :], x_in[tok0:tok0 + 128, :]), 16)
                bk.sync(dmain)

                _ln_emit(nc, bk, ch, xtile, xn_tm, lnscr, lnm, lns, lnr, eps_col)

                pe0, ev0 = pe_s.n, ev_s.n

                def trans(t, pe0=pe0, ev0=ev0):
                    ch.wait(t)
                    for j in range(8):
                        if j >= 2:
                            ev_s.wait(t, ev0 + (j - 2) + 1)
                        pe_s.push(t.transpose(ps_tr[j % 2][:, 0:128],
                                              xn_tm[:, j * 128:(j + 1) * 128], id_bf[:, :]))
                bk.tensor(trans)

                def evac(v, it=it, pe0=pe0, ev0=ev0):
                    for j in range(8):
                        pe_s.wait(v, pe0 + 1 + j)
                        ev_s.push(v.tensor_copy(xq_fm[:, j, it * 128:(it + 1) * 128],
                                                ps_tr[j % 2][:, 0:128]))
                bk.vector(evac)

                def joinv(v):
                    ev_s.wait(v)
                    ch.push(v.tensor_copy(lnm[:, :], lnm[:, :]))
                bk.vector(joinv)

                def rmm(t, it=it):
                    ch.wait(t)
                    for k in range(8):
                        inst = t.matmul(ps_r[:, :], xq_fm[:, k, it * 128:(it + 1) * 128],
                                        wr_sb[:, k, :], start=(k == 0), stop=(k == 7))
                    ch.push(inst)
                bk.tensor(rmm)

                def rsoft1(v):
                    ch.wait(v)
                    t1 = ch.push(v.tensor_copy(r_pr[:, :], ps_r[:, :]))
                    ch.wait(v, t1)
                    t2 = ch.push(v.tensor_reduce(r_m1[:, :], r_pr[:, :], AX.X, ALU.max))
                    ch.wait(v, t2)
                    ch.push(v.tensor_scalar(r_pr[:, :], r_pr[:, :], r_m1[:, :], None, ALU.subtract))
                bk.vector(rsoft1)

                def rexp(s):
                    ch.wait(s)
                    ch.push(s.activation(r_pr[:, :], r_pr[:, :], AF.Exp))
                bk.scalar(rexp)

                def rsoft2(v, it=it):
                    ch.wait(v)
                    seq = [
                        lambda: v.tensor_reduce(r_den[:, :], r_pr[:, :], AX.X, ALU.add),
                        lambda: v.reciprocal(r_den[:, :], r_den[:, :]),
                        lambda: v.tensor_scalar(r_pr[:, :], r_pr[:, :], r_den[:, :], None, ALU.mult),
                        lambda: v.tensor_reduce(r_m1[:, :], r_pr[:, :], AX.X, ALU.max),
                        lambda: v.tensor_scalar(r_k1[:, :], r_pr[:, :], r_m1[:, :], None, ALU.is_ge),
                        lambda: v.tensor_tensor(r_t[:, :], r_k1[:, :], r_pr[:, :], ALU.mult),
                        lambda: v.tensor_tensor(r_t[:, :], r_pr[:, :], r_t[:, :], ALU.subtract),
                        lambda: v.tensor_reduce(r_m2[:, :], r_t[:, :], AX.X, ALU.max),
                        lambda: v.tensor_scalar(r_k2[:, :], r_t[:, :], r_m2[:, :], None, ALU.is_ge),
                        lambda: v.tensor_tensor(r_k1[:, :], r_k1[:, :], r_k2[:, :], ALU.add),
                        lambda: v.tensor_scalar(r_den[:, :], r_m1[:, :], r_m2[:, :], 1e-8, ALU.add, ALU.add),
                        lambda: v.reciprocal(r_den[:, :], r_den[:, :]),
                        lambda: v.tensor_tensor(r_t[:, :], r_pr[:, :], r_k1[:, :], ALU.mult),
                        lambda: v.tensor_scalar(w_tm[:, it, :], r_t[:, :], r_den[:, :], None, ALU.mult),
                        lambda: v.tensor_copy(w_bf[:, :], w_tm[:, it, :]),
                    ]
                    for f in seq:
                        tk = ch.push(f())
                        ch.wait(v, tk)
                bk.vector(rsoft2)

                def wtr(t):
                    ch.wait(t)
                    ch.push(t.transpose(ps_tr[0][0:8, 0:128], w_bf[:, :], id_bf[:, :]))
                bk.tensor(wtr)

                def wev(v, it=it):
                    ch.wait(v)
                    ch.push(v.tensor_copy(w_fmT[0:8, it * 128:(it + 1) * 128],
                                          ps_tr[0][0:8, 0:128]))
                bk.vector(wev)

            for it in range(NTT):
                def bterm(t, it=it):
                    ch.wait(t)
                    for hh in range(2):
                        inst = t.matmul(ps_o[0][:, hh * 512:(hh + 1) * 512],
                                        w_fmT[:, it * 128:(it + 1) * 128],
                                        bo_sb[:, hh * 512:(hh + 1) * 512],
                                        start=True, stop=True)
                    ch.push(inst, 1)
                bk.tensor(bterm)

                def bev(v, it=it):
                    ch.wait(v)
                    ch.push(v.tensor_copy(acc[:, it, :], ps_o[0][:, :]))
                bk.vector(bev)

        if dbg and q == 0:
            with nc.Block() as bk:
                def dump1(g):
                    ch.wait(g)
                    outs.push(g.dma_start(d_xq[:, :, :], xq_fm[:, :, :]), 16)
                    outs.push(g.dma_start(d_w[:, :, :], w_tm[:, :, :]), 16)
                    outs.push(g.dma_start(d_acc0[:, :, :], acc[:, :, :]), 16)
                bk.gpsimd(dump1)

        for e in range(8):
            with nc.Block() as bk:
                def wload(g, e=e):
                    wdma.push(g.dma_start(wg_sb[:, :, :],
                                          wg_d[e].rearrange("(k p) m -> p k m", p=128)), 16)
                    wdma.push(g.dma_start(wu_sb[:, :, :],
                                          wu_d[e].rearrange("(k p) m -> p k m", p=128)), 16)
                    wdma.push(g.dma_start(wo_sb[:, :, :],
                                          wo_d[e].rearrange("(k p) m -> p k m", p=128)), 16)
                bk.gpsimd(wload)

                for h in range(2):
                    tok0 = h * 512
                    for m in range(16):
                        def gumm(t, e=e, m=m, tok0=tok0):
                            wdma.wait(t)
                            ch.wait(t)
                            for k in range(8):
                                ig = t.matmul(ps_g[:, :], wg_sb[:, k, m * 128:(m + 1) * 128],
                                              xq_fm[:, k, tok0:tok0 + 512],
                                              start=(k == 0), stop=(k == 7))
                            ch.push(ig)
                            for k in range(8):
                                iu = t.matmul(ps_u[:, :], wu_sb[:, k, m * 128:(m + 1) * 128],
                                              xq_fm[:, k, tok0:tok0 + 512],
                                              start=(k == 0), stop=(k == 7))
                            ch.push(iu)
                        bk.tensor(gumm)

                        def gsilu(s, e=e, m=m):
                            ch.wait(s)
                            ch.push(s.activation(gs_sb[:, :], ps_g[:, :], AF.Silu,
                                                 bias=bg_sb[:, m, e:e + 1]))
                        bk.scalar(gsilu)

                        def hmul(v, e=e, m=m):
                            ch.wait(v)
                            ch.push(v.scalar_tensor_tensor(hw_sb[:, m, :], ps_u[:, :],
                                                           bu_sb[:, m, e:e + 1],
                                                           gs_sb[:, :], ALU.add, ALU.mult))
                        bk.vector(hmul)

                    for tt in range(4):
                        it = h * 4 + tt
                        pso = ps_o[tt % 2]

                        def omm(t, tt=tt, pso=pso):
                            ch.wait(t)
                            for k in range(16):
                                for half in range(2):
                                    inst = t.matmul(pso[:, half * 512:(half + 1) * 512],
                                                    hw_sb[:, k, tt * 128:(tt + 1) * 128],
                                                    wo_sb[:, k, half * 512:(half + 1) * 512],
                                                    start=(k == 0), stop=(k == 15))
                            ch.push(inst)
                        bk.tensor(omm)

                        def oacc(v, e=e, it=it, pso=pso):
                            ch.wait(v)
                            ch.push(v.scalar_tensor_tensor(acc[:, it, :], pso[:, :],
                                                           w_tm[:, it, e:e + 1], acc[:, it, :],
                                                           ALU.mult, ALU.add))
                        bk.vector(oacc)
                if dbg and q == 0 and e == 0:
                    def dump3(g):
                        ch.wait(g)
                        outs.push(g.dma_start(d_acc1[:, :, :], acc[:, :, :]), 16)
                    bk.gpsimd(dump3)

        if dbg and q == 0:
            with nc.Block() as bk:
                def dump2(g):
                    ch.wait(g)
                    outs.push(g.dma_start(d_acc[:, :, :], acc[:, :, :]), 16)
                    outs.push(g.dma_start(d_hw[:, :, :], hw_sb[:, :, :]), 16)
                bk.gpsimd(dump2)

        with nc.Block() as bk:
            for it in range(NTT):
                tok0 = q * QT + it * 128

                def dmain(sy, tok0=tok0):
                    ch.wait(sy)
                    ch.push(sy.dma_start(xtile[:, :], x_in[tok0:tok0 + 128, :]), 16)
                bk.sync(dmain)

                def radd(v, it=it):
                    ch.wait(v)
                    outs.wait(v)
                    ch.push(v.tensor_tensor(o_tm[:, :], acc[:, it, :], xtile[:, :], ALU.add))
                bk.vector(radd)

                def dmaout(sy, tok0=tok0):
                    ch.wait(sy)
                    outs.push(sy.dma_start(x_out[tok0:tok0 + 128, :], o_tm[:, :]), 16)
                bk.sync(dmaout)

            def fin(sy):
                outs.wait(sy)
            bk.sync(fin)

    return nc


_CACHE = {}


def _get(name, builder):
    if name not in _CACHE:
        nc = builder()
        nc.finalize()
        _CACHE[name] = nc
    return _CACHE[name]


def run_moe(x, inputs):
    nc = _get("moe", build_moe)
    ident = np.eye(128, dtype=np.float32)
    in_maps = []
    for c in range(NCORE):
        xs = np.ascontiguousarray(x[2 * c:2 * c + 2].reshape(TOK, DM))
        in_maps.append({
            "x_in": xs,
            "wr": inputs["L0_ffn_W_router"],
            "wg": inputs["L0_ffn_W_gate"],
            "bg": np.ascontiguousarray(np.asarray(inputs["L0_ffn_b_gate"]).reshape(8, 16, 128).transpose(2, 1, 0)),
            "wu": inputs["L0_ffn_W_up"],
            "bu": np.ascontiguousarray(np.asarray(inputs["L0_ffn_b_up"]).reshape(8, 16, 128).transpose(2, 1, 0)),
            "wo": inputs["L0_ffn_W_out"],
            "bo": inputs["L0_ffn_b_out"],
            "ident": ident,
        })
    res = run_bass_kernel_spmd(nc, in_maps, core_ids=list(range(NCORE)))
    out = np.concatenate([r["x_out"].reshape(BPC, T, DM) for r in res.results], axis=0)
    return out


def kernel(**inputs):
    x = inputs["L0_attn_x"].astype(np.float32, copy=False)
    x1 = run_mamba(x, inputs)
    x2 = run_moe(x1, inputs)
    x3 = run_ttt(x2, inputs)
    x4 = run_lstm(x3, inputs)
    return x4




def build_mamba(dbg=False):
    nc = bass.Bass("TRN2", target_bir_lowering=False, debug=False)
    TC = 512
    NCH = T // TC
    NG = DI // 8

    x_in = nc.dram_tensor("x_in", [BPC, T, DM], F32, kind="ExternalInput").ap()
    inw_d = nc.dram_tensor("in_w", [DM, 2 * DI], F32, kind="ExternalInput").ap()
    inb_d = nc.dram_tensor("in_b", [128, 32], F32, kind="ExternalInput").ap()
    cw_d = nc.dram_tensor("conv_w", [128, 16, 4], F32, kind="ExternalInput").ap()
    cb_d = nc.dram_tensor("conv_b", [128, 16], F32, kind="ExternalInput").ap()
    ap_d = nc.dram_tensor("A_perm", [128, 256], F32, kind="ExternalInput").ap()
    dp_d = nc.dram_tensor("Dp_col", [128, 16], F32, kind="ExternalInput").ap()
    dtw_d = nc.dram_tensor("dt_w", [DI, DI], F32, kind="ExternalInput").ap()
    dtb_d = nc.dram_tensor("dt_b", [128, 16], F32, kind="ExternalInput").ap()
    bcw_d = nc.dram_tensor("bc_w", [DI, 32], F32, kind="ExternalInput").ap()
    bcb_d = nc.dram_tensor("bc_b", [32, 1], F32, kind="ExternalInput").ap()
    ow_d = nc.dram_tensor("out_w", [DI, DM], F32, kind="ExternalInput").ap()
    ob_d = nc.dram_tensor("out_b", [128, 8], F32, kind="ExternalInput").ap()
    sel_d = nc.dram_tensor("SEL", [128, 16, 128], F32, kind="ExternalInput").ap()
    red_d = nc.dram_tensor("RED", [128, 16, 128], F32, kind="ExternalInput").ap()
    sbc_d = nc.dram_tensor("selbc", [32, 2, 128], F32, kind="ExternalInput").ap()
    id_d = nc.dram_tensor("ident", [128, 128], F32, kind="ExternalInput").ap()
    x_out = nc.dram_tensor("x_out", [BPC, T, DM], F32, kind="ExternalOutput").ap()

    xz_sc = nc.dram_tensor("xz_sc", [BPC, 16, 128, T], BF16, kind="Internal").ap()
    gt_sc = nc.dram_tensor("gt_sc", [BPC, 16, 128, T], BF16, kind="Internal").ap()

    inb_sb = nc.alloc_sbuf_tensor("inb_sb", [128, 32], F32)
    cw_sb = nc.alloc_sbuf_tensor("cw_sb", [128, 16, 4], F32)
    cb_sb = nc.alloc_sbuf_tensor("cb_sb", [128, 16], F32)
    ap_sb = nc.alloc_sbuf_tensor("ap_sb", [128, 256], F32)
    dp_sb = nc.alloc_sbuf_tensor("dp_sb", [128, 16], F32)
    dtb_sb = nc.alloc_sbuf_tensor("dtb_sb", [128, 16], F32)
    bcb_sb = nc.alloc_sbuf_tensor("bcb_sb", [32, 1], F32)
    ob_sb = nc.alloc_sbuf_tensor("ob_sb", [128, 8], F32)
    sel_sb = nc.alloc_sbuf_tensor("sel_sb", [128, 16, 128], BF16)
    red_sb = nc.alloc_sbuf_tensor("red_sb", [128, 16, 128], BF16)
    sbc_sb = nc.alloc_sbuf_tensor("sbc_sb", [32, 2, 128], BF16)
    id_bf = nc.alloc_sbuf_tensor("id_bf", [128, 128], BF16)
    eps_col = nc.alloc_sbuf_tensor("eps_col", [128, 1], F32)
    A1 = nc.alloc_sbuf_tensor("m_A1", [128, 8 * 2 * DI], BF16)
    inw_sb = A1[:, :].rearrange("p (k m) -> p k m", k=8)
    dtw_sb = A1[:, :].rearrange("p (k m) -> p k m", k=16)
    ow_sb = nc.alloc_sbuf_tensor("m_ow", [128, 16, DM], BF16)
    bcw_sb = nc.alloc_sbuf_tensor("m_bcw", [128, 16, 32], BF16)
    ar1 = nc.alloc_sbuf_tensor("m_ar1", [128, 16, TC + 3], BF16)
    xz_fm = ar1[:, :, 0:TC]
    xzh = ar1[:, :, :]
    yg_fm = ar1[:, :, 0:TC]
    gtc_fm = nc.alloc_sbuf_tensor("m_gt_fm", [128, 16, TC], BF16)
    xc_fm = nc.alloc_sbuf_tensor("m_xc_fm", [128, 16, TC], BF16)
    dt_fm = nc.alloc_sbuf_tensor("m_dt_fm", [128, 16, TC], BF16)
    arof = nc.alloc_sbuf_tensor("m_arof", [128, 8, TC], BF16)
    xn_fm = arof[:, :, :]
    of_fm = arof[:, :, :]
    cvr = nc.alloc_sbuf_tensor("m_cvr", [128, TC], F32)
    sp_m = cvr[:, :]
    dtxJ = nc.alloc_sbuf_tensor("m_dtxJ", [128, TC], BF16)
    xtile = nc.alloc_sbuf_tensor("m_xtile", [128, DM], F32)
    xn_tm = nc.alloc_sbuf_tensor("m_xn_tm", [128, DM], BF16)
    lnscr = nc.alloc_sbuf_tensor("m_lnscr", [128, DM], BF16)
    lnm = nc.alloc_sbuf_tensor("m_lnm", [128, 1], F32)
    lns = nc.alloc_sbuf_tensor("m_lns", [128, 1], F32)
    lnr = nc.alloc_sbuf_tensor("m_lnr", [128, 1], F32)
    sp_a = nc.alloc_sbuf_tensor("m_sp_a", [128, TC], BF16)
    bc_sb = nc.alloc_sbuf_tensor("m_bc_sb", [32, TC], BF16)
    brep = nc.alloc_sbuf_tensor("m_brep", [128, TC], BF16)
    crep = nc.alloc_sbuf_tensor("m_crep", [128, TC], BF16)
    dA_sb = nc.alloc_sbuf_tensor("m_dA", [128, TC], F32)
    u_sb = nc.alloc_sbuf_tensor("m_u", [128, TC], F32)
    h_sb = nc.alloc_sbuf_tensor("m_h", [128, TC], BF16)
    hc_sb = nc.alloc_sbuf_tensor("m_hc", [128, TC], BF16)
    hcar = nc.alloc_sbuf_tensor("m_hcar", [128, 256], F32)
    sg_sb = nc.alloc_sbuf_tensor("m_sg", [128, TC], BF16)
    t1_sb = nc.alloc_sbuf_tensor("m_t1", [128, TC], BF16)
    o_tm = nc.alloc_sbuf_tensor("m_otm", [128, DM], F32)

    ps_mm = nc.alloc_psum_tensor("mps_mm", [128, 512], F32)
    ps_bc = nc.alloc_psum_tensor("mps_bc", [128, 512], F32)
    ps_dtr = nc.alloc_psum_tensor("mps_dtr", [128, 512], F32)
    ps_dxr = nc.alloc_psum_tensor("mps_dxr", [128, 512], F32)
    ps_y = nc.alloc_psum_tensor("mps_y", [128, 512], F32)
    ps_tr = [nc.alloc_psum_tensor(f"mps_tr{i}", [128, 1024], BF16) for i in range(2)]

    ch = Chain(nc, "m_ch")
    pe_s = Chain(nc, "m_pe")
    ev_s = Chain(nc, "m_ev")
    dch = Chain(nc, "m_dch")
    wdma = Chain(nc, "m_wdma")
    outs = Chain(nc, "m_outs")

    with nc.Block() as bk:
        def gsec(g):
            dch.push(g.dma_start(inb_sb[:, :], inb_d), 16)
            dch.push(g.dma_start(cw_sb[:, :, :], cw_d), 16)
            dch.push(g.dma_start(cb_sb[:, :], cb_d), 16)
            dch.push(g.dma_start(ap_sb[:, :], ap_d), 16)
            dch.push(g.dma_start(dp_sb[:, :], dp_d), 16)
            dch.push(g.dma_start(dtb_sb[:, :], dtb_d), 16)
            dch.push(g.dma_start(bcb_sb[:, :], bcb_d), 16)
            dch.push(g.dma_start(ob_sb[:, :], ob_d), 16)
            dch.push(g.dma_start(sel_sb[:, :, :], sel_d), 16)
            dch.push(g.dma_start(red_sb[:, :, :], red_d), 16)
            dch.push(g.dma_start(sbc_sb[:, :, :], sbc_d), 16)
            dch.push(g.dma_start(id_bf[:, :], id_d), 16)
            dch.push(g.dma_start(bcw_sb[:, :, :], bcw_d.rearrange("(k p) c -> p k c", p=128)), 16)
            dch.push(g.dma_start(inw_sb, inw_d.rearrange("(k p) m -> p k m", p=128)), 16)
            g.memset(eps_col[:, :], EPS)
            g.memset(hcar[:, :], 0.0)
        bk.gpsimd(gsec)

    for b in range(BPC):
        for c in range(NCH):
            t0 = c * TC
            with nc.Block() as bk:
                for tt in range(4):
                    def dmain(sy, b=b, t0=t0, tt=tt):
                        ch.wait(sy)
                        dch.push(sy.dma_start(xtile[:, :], x_in[b, t0 + tt * 128:t0 + (tt + 1) * 128, :]), 16)
                    bk.sync(dmain)

                    def vstart(v):
                        dch.wait(v)
                        ch.push(v.tensor_copy(lnm[:, :], eps_col[:, :]))
                    bk.vector(vstart)

                    _ln_emit(nc, bk, ch, xtile, xn_tm, lnscr, lnm, lns, lnr, eps_col)

                    pe0, ev0 = pe_s.n, ev_s.n

                    def trans(t, pe0=pe0, ev0=ev0):
                        ch.wait(t)
                        for j in range(8):
                            if j >= 2:
                                ev_s.wait(t, ev0 + (j - 2) + 1)
                            pe_s.push(t.transpose(ps_tr[j % 2][:, 0:128],
                                                  xn_tm[:, j * 128:(j + 1) * 128], id_bf[:, :]))
                    bk.tensor(trans)

                    def evac(v, tt=tt, pe0=pe0, ev0=ev0):
                        for j in range(8):
                            pe_s.wait(v, pe0 + 1 + j)
                            ev_s.push(v.tensor_copy(xn_fm[:, j, tt * 128:(tt + 1) * 128],
                                                    ps_tr[j % 2][:, 0:128]))
                    bk.vector(evac)

                    def joinv(v):
                        ev_s.wait(v)
                        ch.push(v.tensor_copy(lnm[:, :], lnm[:, :]))
                    bk.vector(joinv)

                for m in range(32):
                    def ipmm(t, m=m):
                        ch.wait(t)
                        for k in range(8):
                            inst = t.matmul(ps_mm[:, :], inw_sb[:, k, m * 128:(m + 1) * 128],
                                            xn_fm[:, k, :], start=(k == 0), stop=(k == 7))
                        ch.push(inst)
                    bk.tensor(ipmm)

                    def ipev(v, m=m):
                        ch.wait(v)
                        dst = xz_fm[:, m, :] if m < 16 else gtc_fm[:, m - 16, :]
                        ch.push(v.tensor_scalar(dst, ps_mm[:, :], inb_sb[:, m:m + 1], None, ALU.add))
                    bk.vector(ipev)

                def st(g, b=b, t0=t0):
                    ch.wait(g)
                    outs.push(g.dma_start(xz_sc[b, :, :, t0:t0 + TC], xz_fm[:, :, :]), 16)
                    outs.push(g.dma_start(gt_sc[b, :, :, t0:t0 + TC], gtc_fm[:, :, :]), 16)
                bk.gpsimd(st)

                def fin(sy):
                    outs.wait(sy)
                bk.sync(fin)

    with nc.Block() as bk:
        def wload(g):
            wdma.push(g.dma_start(dtw_sb, dtw_d.rearrange("(k p) m -> p k m", p=128)), 16)
            wdma.push(g.dma_start(ow_sb[:, :, :], ow_d.rearrange("(k p) m -> p k m", p=128)), 16)
        bk.gpsimd(wload)

    for b in range(BPC):
        with nc.Block() as bk:
            def hrst(v, b=b):
                ch.push(v.tensor_scalar(hcar[:, :], hcar[:, :], 0.0, None, ALU.mult))
            bk.vector(hrst)

        for c in range(NCH):
            t0 = c * TC
            with nc.Block() as bk:
                lo = max(t0 - 1, 0)
                hi = min(t0 + TC + 2, T)
                off = lo - (t0 - 1)

                def ld(g, b=b, t0=t0, lo=lo, hi=hi, off=off):
                    g.memset(ar1[:, :, :], 0.0)
                    dch.push(g.dma_start(xzh[:, :, off:off + hi - lo], xz_sc[b, :, :, lo:hi]), 16)
                    dch.push(g.dma_start(gtc_fm[:, :, :], gt_sc[b, :, :, t0:t0 + TC]), 16)
                bk.gpsimd(ld)

                for j in range(16):
                    def conv(v, j=j):
                        if j == 0:
                            dch.wait(v)
                            wdma.wait(v)
                        ch.wait(v)
                        t1 = ch.push(v.tensor_scalar(cvr[:, :], xzh[:, j, 0:TC],
                                                     cw_sb[:, j, 0:1], None, ALU.mult))
                        for jj in range(1, 4):
                            ch.wait(v, t1)
                            t1 = ch.push(v.scalar_tensor_tensor(cvr[:, :], xzh[:, j, jj:jj + TC],
                                                                cw_sb[:, j, jj:jj + 1], cvr[:, :],
                                                                ALU.mult, ALU.add))
                    bk.vector(conv)

                    def csil(s, j=j):
                        ch.wait(s)
                        ch.push(s.activation(xc_fm[:, j, :], cvr[:, :], AF.Silu,
                                             bias=cb_sb[:, j:j + 1]))
                    bk.scalar(csil)

                def bcmm(t):
                    ch.wait(t)
                    for k in range(16):
                        inst = t.matmul(ps_bc[0:32, :], bcw_sb[:, k, :], xc_fm[:, k, :],
                                        start=(k == 0), stop=(k == 15))
                    ch.push(inst)
                bk.tensor(bcmm)

                def bcev(v):
                    ch.wait(v)
                    ch.push(v.tensor_scalar(bc_sb[:, :], ps_bc[0:32, :], bcb_sb[:, :], None, ALU.add))
                bk.vector(bcev)

                def repmm(t):
                    ch.wait(t)
                    ch.push(t.matmul(ps_bc[:, :], sbc_sb[:, 0, :], bc_sb[:, :], start=True, stop=True))
                bk.tensor(repmm)

                def repev(v):
                    ch.wait(v)
                    ch.push(v.tensor_copy(brep[:, :], ps_bc[:, :]))
                bk.vector(repev)

                def repmm2(t):
                    ch.wait(t)
                    ch.push(t.matmul(ps_bc[:, :], sbc_sb[:, 1, :], bc_sb[:, :], start=True, stop=True))
                bk.tensor(repmm2)

                def repev2(v):
                    ch.wait(v)
                    ch.push(v.tensor_copy(crep[:, :], ps_bc[:, :]))
                bk.vector(repev2)

                for m in range(16):
                    def dtmm(t, m=m):
                        ch.wait(t)
                        for k in range(16):
                            inst = t.matmul(ps_mm[:, :], dtw_sb[:, k, m * 128:(m + 1) * 128],
                                            xc_fm[:, k, :], start=(k == 0), stop=(k == 15))
                        ch.push(inst)
                    bk.tensor(dtmm)

                    def dtsp1(s, m=m):
                        ch.wait(s)
                        t1 = ch.push(s.activation(sp_a[:, :], ps_mm[:, :], AF.Abs,
                                                  bias=dtb_sb[:, m:m + 1]))
                        ch.wait(s, t1)
                        t2 = ch.push(s.activation(sp_a[:, :], sp_a[:, :], AF.Exp, scale=-1.0))
                        ch.wait(s, t2)
                        ch.push(s.activation(sp_a[:, :], sp_a[:, :], AF.Ln, bias=1.0))
                    bk.scalar(dtsp1)

                    def dtsp2(v, m=m):
                        ch.wait(v)
                        t1 = ch.push(v.tensor_scalar(sp_m[:, :], ps_mm[:, :], dtb_sb[:, m:m + 1],
                                                     0.0, ALU.add, ALU.max))
                        ch.wait(v, t1)
                        ch.push(v.tensor_tensor(dt_fm[:, m, :], sp_m[:, :], sp_a[:, :], ALU.add))
                    bk.vector(dtsp2)

                for J in range(16):
                    def dtxm(v, J=J):
                        ch.wait(v)
                        ch.push(v.tensor_tensor(dtxJ[:, :], dt_fm[:, J, :], xc_fm[:, J, :], ALU.mult))
                    bk.vector(dtxm)

                    for j in range(16):
                        g16 = J * 16 + j

                        def smm(t, J=J, j=j):
                            ch.wait(t)
                            ch.push(t.matmul(ps_dtr[:, :], sel_sb[:, j, :], dt_fm[:, J, :],
                                             start=True, stop=True))
                            ch.push(t.matmul(ps_dxr[:, :], sel_sb[:, j, :], dtxJ[:, :],
                                             start=True, stop=True))
                        bk.tensor(smm)

                        def sda(s, g16=g16):
                            ch.wait(s)
                            ch.push(s.activation(dA_sb[:, :], ps_dtr[:, :], AF.Exp,
                                                 scale=ap_sb[:, g16:g16 + 1]))
                        bk.scalar(sda)

                        def svec(v, g16=g16):
                            ch.wait(v)
                            t1 = ch.push(v.tensor_tensor(u_sb[:, :], ps_dxr[:, :], brep[:, :], ALU.mult))
                            ch.wait(v, t1)
                            t2 = ch.push(v.tensor_tensor_scan(h_sb[:, :], dA_sb[:, :], u_sb[:, :],
                                                              hcar[:, g16:g16 + 1], ALU.mult, ALU.add))
                            ch.wait(v, t2)
                            t3 = ch.push(v.tensor_copy(hcar[:, g16:g16 + 1], h_sb[:, TC - 1:TC]))
                            ch.wait(v, t3)
                            ch.push(v.tensor_tensor(hc_sb[:, :], h_sb[:, :], crep[:, :], ALU.mult))
                        bk.vector(svec)

                        def ymm(t, j=j):
                            ch.wait(t)
                            ch.push(t.matmul(ps_y[:, :], red_sb[:, j, :], hc_sb[:, :],
                                             start=(j == 0), stop=(j == 15)))
                        bk.tensor(ymm)

                    def gat1(s, J=J):
                        ch.wait(s)
                        ch.push(s.activation(sg_sb[:, :], gtc_fm[:, J, :], AF.Silu))
                    bk.scalar(gat1)

                    def gat2(v, J=J):
                        ch.wait(v)
                        t1 = ch.push(v.scalar_tensor_tensor(t1_sb[:, :], xc_fm[:, J, :],
                                                            dp_sb[:, J:J + 1], ps_y[:, :],
                                                            ALU.mult, ALU.add))
                        ch.wait(v, t1)
                        ch.push(v.tensor_tensor(yg_fm[:, J, :], t1_sb[:, :], sg_sb[:, :], ALU.mult))
                    bk.vector(gat2)

                for m in range(8):
                    def omm(t, m=m):
                        ch.wait(t)
                        for k in range(16):
                            inst = t.matmul(ps_mm[:, :], ow_sb[:, k, m * 128:(m + 1) * 128],
                                            yg_fm[:, k, :], start=(k == 0), stop=(k == 15))
                        ch.push(inst)
                    bk.tensor(omm)

                    def oev(v, m=m):
                        ch.wait(v)
                        ch.push(v.tensor_scalar(of_fm[:, m, :], ps_mm[:, :], ob_sb[:, m:m + 1],
                                                None, ALU.add))
                    bk.vector(oev)

                for tt in range(4):
                    def dmx(sy, b=b, t0=t0, tt=tt):
                        ch.wait(sy)
                        outs.wait(sy)
                        dch.push(sy.dma_start(xtile[:, :], x_in[b, t0 + tt * 128:t0 + (tt + 1) * 128, :]), 16)
                    bk.sync(dmx)

                    pe0, ev0 = pe_s.n, ev_s.n

                    def etr(t, tt=tt, pe0=pe0, ev0=ev0):
                        ch.wait(t)
                        for m in range(8):
                            if m >= 2:
                                ev_s.wait(t, ev0 + (m - 2) + 1)
                            pe_s.push(t.transpose(ps_tr[m % 2][:, 0:128],
                                                  of_fm[:, m, tt * 128:(tt + 1) * 128], id_bf[:, :]))
                    bk.tensor(etr)

                    def eev(v, tt=tt, pe0=pe0, ev0=ev0):
                        dch.wait(v)
                        for m in range(8):
                            pe_s.wait(v, pe0 + 1 + m)
                            ev_s.push(v.tensor_tensor(o_tm[:, m * 128:(m + 1) * 128],
                                                      ps_tr[m % 2][:, 0:128],
                                                      xtile[:, m * 128:(m + 1) * 128], ALU.add))
                    bk.vector(eev)

                    def ejoin(v):
                        ev_s.wait(v)
                        ch.push(v.tensor_copy(lnm[:, :], lnm[:, :]))
                    bk.vector(ejoin)

                    def dmo(sy, b=b, t0=t0, tt=tt):
                        ch.wait(sy)
                        outs.push(sy.dma_start(x_out[b, t0 + tt * 128:t0 + (tt + 1) * 128, :],
                                               o_tm[:, :]), 16)
                    bk.sync(dmo)

                def fin(sy):
                    outs.wait(sy)
                bk.sync(fin)

    return nc


def run_mamba(x, inputs):
    nc = _get("mamba", build_mamba)
    ident = np.eye(128, dtype=np.float32)
    A = np.asarray(inputs["L0_attn_A"])
    A_perm = np.ascontiguousarray(A.reshape(256, 128).T)
    SEL = np.zeros((128, 16, 128), np.float32)
    for r in range(128):
        for jj in range(16):
            for p in range(128):
                if r == 8 * jj + p // 16:
                    SEL[r, jj, p] = 1.0
    RED = np.ascontiguousarray(SEL.transpose(2, 1, 0))
    selbc = np.zeros((32, 2, 128), np.float32)
    for p in range(128):
        selbc[p % 16, 0, p] = 1.0
        selbc[16 + p % 16, 1, p] = 1.0
    conv_w = np.asarray(inputs["L0_attn_conv_w"]).reshape(4, DI)
    base = {
        "in_w": inputs["L0_attn_in_w"],
        "in_b": np.ascontiguousarray(np.asarray(inputs["L0_attn_in_b"]).reshape(32, 128).T),
        "conv_w": np.ascontiguousarray(conv_w.reshape(4, 16, 128).transpose(2, 1, 0)),
        "conv_b": np.ascontiguousarray(np.asarray(inputs["L0_attn_conv_b"]).reshape(16, 128).T),
        "A_perm": A_perm,
        "Dp_col": np.ascontiguousarray(np.asarray(inputs["L0_attn_Dp"]).reshape(16, 128).T),
        "dt_w": inputs["L0_attn_dt_w"],
        "dt_b": np.ascontiguousarray(np.asarray(inputs["L0_attn_dt_b"]).reshape(16, 128).T),
        "bc_w": np.ascontiguousarray(np.concatenate([np.asarray(inputs["L0_attn_B_w"]),
                                                     np.asarray(inputs["L0_attn_C_w"])], axis=1)),
        "bc_b": np.ascontiguousarray(np.concatenate([np.asarray(inputs["L0_attn_B_b"]),
                                                     np.asarray(inputs["L0_attn_C_b"])]).reshape(32, 1)),
        "out_w": inputs["L0_attn_out_w"],
        "out_b": np.ascontiguousarray(np.asarray(inputs["L0_attn_out_b"]).reshape(8, 128).T),
        "SEL": SEL,
        "RED": RED,
        "selbc": selbc,
        "ident": ident,
    }
    in_maps = []
    for cc in range(NCORE):
        m = dict(base)
        m["x_in"] = np.ascontiguousarray(x[2 * cc:2 * cc + 2])
        in_maps.append(m)
    res = run_bass_kernel_spmd(nc, in_maps, core_ids=list(range(NCORE)))
    return np.concatenate([r["x_out"] for r in res.results], axis=0)




def build_ttt():
    nc = bass.Bass("TRN2", target_bir_lowering=False, debug=False)
    CK = 128
    NCK = T // CK
    LR = 0.01
    ALPHA = 2.0 / DM
    LAM = 2.0 * LR / DM

    x_in = nc.dram_tensor("x_in", [BPC, T, DM], F32, kind="ExternalInput").ap()
    thk_d = nc.dram_tensor("thK_T", [DM, DM], F32, kind="ExternalInput").ap()
    thv_d = nc.dram_tensor("thV_T", [DM, DM], F32, kind="ExternalInput").ap()
    thq_d = nc.dram_tensor("thQ_T", [DM, DM], F32, kind="ExternalInput").ap()
    w0t_d = nc.dram_tensor("W0_T", [DM, DM], F32, kind="ExternalInput").ap()
    b0_d = nc.dram_tensor("b0", [1, DM], F32, kind="ExternalInput").ap()
    mst_d = nc.dram_tensor("Mstrict", [128, 128], F32, kind="ExternalInput").ap()
    min_d = nc.dram_tensor("Mincl", [128, 128], F32, kind="ExternalInput").ap()
    id_d = nc.dram_tensor("ident", [128, 128], F32, kind="ExternalInput").ap()
    x_out = nc.dram_tensor("x_out", [BPC, T, DM], F32, kind="ExternalOutput").ap()

    thk = nc.alloc_sbuf_tensor("t_thk", [128, 8, DM], BF16)
    thq = nc.alloc_sbuf_tensor("t_thq", [128, 8, DM], BF16)
    thvn = nc.alloc_sbuf_tensor("t_thvn", [128, 8, DM], BF16)
    wt = [nc.alloc_sbuf_tensor(f"t_wt{b}", [128, 8, DM], BF16) for b in range(BPC)]
    bst = nc.alloc_sbuf_tensor("t_bst", [1, DM], F32)
    bst_bf = nc.alloc_sbuf_tensor("t_bstbf", [1, DM], BF16)
    mstr = nc.alloc_sbuf_tensor("t_mstr", [128, 128], BF16)
    mincl = nc.alloc_sbuf_tensor("t_mincl", [128, 128], BF16)
    id_bf = nc.alloc_sbuf_tensor("t_id", [128, 128], BF16)
    ones_r = nc.alloc_sbuf_tensor("t_ones_r", [1, 128], BF16)
    ones_c = nc.alloc_sbuf_tensor("t_ones_c", [128, 1], BF16)
    eps_col = nc.alloc_sbuf_tensor("t_eps", [128, 1], F32)
    kfm = nc.alloc_sbuf_tensor("t_kfm", [128, 8, CK], BF16)
    qfm = nc.alloc_sbuf_tensor("t_qfm", [128, 8, CK], BF16)
    ktm = nc.alloc_sbuf_tensor("t_ktm", [128, 8, 128], BF16)
    base_sb = nc.alloc_sbuf_tensor("t_base", [128, DM], BF16)
    gs_tm = nc.alloc_sbuf_tensor("t_gs", [128, DM], BF16)
    nmat = nc.alloc_sbuf_tensor("t_n", [128, 128], BF16)
    pmat = nc.alloc_sbuf_tensor("t_p", [128, 128], BF16)
    ptmat = nc.alloc_sbuf_tensor("t_pt", [128, 128], BF16)
    tmat = nc.alloc_sbuf_tensor("t_t", [128, 128], BF16)
    umat = nc.alloc_sbuf_tensor("t_u", [128, 128], BF16)
    xn_fm = nc.alloc_sbuf_tensor("t_xnfm", [128, 8, CK], BF16)
    xtile = nc.alloc_sbuf_tensor("t_xtile", [128, DM], F32)
    xn_tm = nc.alloc_sbuf_tensor("t_xntm", [128, DM], BF16)
    lnscr = nc.alloc_sbuf_tensor("t_lnscr", [128, DM], BF16)
    lnm = nc.alloc_sbuf_tensor("t_lnm", [128, 1], F32)
    lns = nc.alloc_sbuf_tensor("t_lns", [128, 1], F32)
    lnr = nc.alloc_sbuf_tensor("t_lnr", [128, 1], F32)
    o_tm = nc.alloc_sbuf_tensor("t_otm", [128, DM], F32)

    ps_big = nc.alloc_psum_tensor("t_psbig", [128, DM], F32)
    ps_sm = nc.alloc_psum_tensor("t_pssm", [128, 512], F32)
    ps_tr = [nc.alloc_psum_tensor(f"t_pstr{i}", [128, 1024], BF16) for i in range(2)]

    ch = Chain(nc, "t_ch")
    pe_s = Chain(nc, "t_pe")
    ev_s = Chain(nc, "t_ev")
    dch = Chain(nc, "t_dch")
    outs = Chain(nc, "t_outs")

    with nc.Block() as bk:
        def gsec(g):
            dch.push(g.dma_start(thk[:, :, :], thk_d.rearrange("(k p) m -> p k m", p=128)), 16)
            dch.push(g.dma_start(thq[:, :, :], thq_d.rearrange("(k p) m -> p k m", p=128)), 16)
            dch.push(g.dma_start(thvn[:, :, :], thv_d.rearrange("(k p) m -> p k m", p=128)), 16)
            for b in range(BPC):
                dch.push(g.dma_start(wt[b][:, :, :], w0t_d.rearrange("(k p) m -> p k m", p=128)), 16)
            dch.push(g.dma_start(mstr[:, :], mst_d), 16)
            dch.push(g.dma_start(mincl[:, :], min_d), 16)
            dch.push(g.dma_start(id_bf[:, :], id_d), 16)
            g.memset(ones_r[:, :], 1.0)
            g.memset(ones_c[:, :], 1.0)
            g.memset(eps_col[:, :], EPS)
        bk.gpsimd(gsec)

        def vneg(s):
            dch.wait(s)
            for j in range(8):
                ch.push(s.activation(thvn[:, j, :], thvn[:, j, :], AF.Identity, scale=-1.0))
        bk.scalar(vneg)

    for b in range(BPC):
        with nc.Block() as bk:
            def binit(g, b=b):
                dch.push(g.dma_start(bst[:, :], b0_d), 16)
            bk.gpsimd(binit)

            def binit2(v):
                dch.wait(v)
                ch.push(v.tensor_copy(bst_bf[:, :], bst[:, :]))
            bk.vector(binit2)

        for c in range(NCK):
            t0 = c * CK
            with nc.Block() as bk:
                def dmain(sy, b=b, t0=t0):
                    ch.wait(sy)
                    dch.push(sy.dma_start(xtile[:, :], x_in[b, t0:t0 + CK, :]), 16)
                bk.sync(dmain)

                def vstart(v):
                    dch.wait(v)
                    ch.push(v.tensor_copy(lnm[:, :], eps_col[:, :]))
                bk.vector(vstart)

                _ln_emit(nc, bk, ch, xtile, xn_tm, lnscr, lnm, lns, lnr, eps_col)

                pe0, ev0 = pe_s.n, ev_s.n

                def trans(t, pe0=pe0, ev0=ev0):
                    ch.wait(t)
                    for j in range(8):
                        if j >= 2:
                            ev_s.wait(t, ev0 + (j - 2) + 1)
                        pe_s.push(t.transpose(ps_tr[j % 2][:, 0:128],
                                              xn_tm[:, j * 128:(j + 1) * 128], id_bf[:, :]))
                bk.tensor(trans)

                def evac(v, pe0=pe0, ev0=ev0):
                    for j in range(8):
                        pe_s.wait(v, pe0 + 1 + j)
                        ev_s.push(v.tensor_copy(xn_fm[:, j, :], ps_tr[j % 2][:, 0:128]))
                bk.vector(evac)

                def joinv(v):
                    ev_s.wait(v)
                    ch.push(v.tensor_copy(lnm[:, :], lnm[:, :]))
                bk.vector(joinv)

                for dst, th in [(kfm, thk), (qfm, thq)]:
                    for i in range(8):
                        def pmm(t, dst=dst, th=th, i=i):
                            ch.wait(t)
                            for k in range(8):
                                inst = t.matmul(ps_sm[:, 0:128], th[:, k, i * 128:(i + 1) * 128],
                                                xn_fm[:, k, :], start=(k == 0), stop=(k == 7))
                            ch.push(inst)
                        bk.tensor(pmm)

                        def pev(v, dst=dst, i=i):
                            ch.wait(v)
                            ch.push(v.tensor_copy(dst[:, i, :], ps_sm[:, 0:128]))
                        bk.vector(pev)

                def basemm(t, b=b):
                    ch.wait(t)
                    for k in range(8):
                        for hh in range(2):
                            t.matmul(ps_big[:, hh * 512:(hh + 1) * 512], kfm[:, k, :],
                                     wt[b][:, k, hh * 512:(hh + 1) * 512],
                                     start=(k == 0), stop=False)
                    for hh in range(2):
                        t.matmul(ps_big[:, hh * 512:(hh + 1) * 512], ones_r[:, :],
                                 bst_bf[0:1, hh * 512:(hh + 1) * 512], start=False, stop=False)
                    for k in range(8):
                        for hh in range(2):
                            inst = t.matmul(ps_big[:, hh * 512:(hh + 1) * 512], xn_fm[:, k, :],
                                            thvn[:, k, hh * 512:(hh + 1) * 512],
                                            start=False, stop=(k == 7))
                    ch.push(inst)
                bk.tensor(basemm)

                def baseev(s):
                    ch.wait(s)
                    ch.push(s.activation(base_sb[:, :], ps_big[:, :], AF.Identity, scale=ALPHA))
                bk.scalar(baseev)

                def kkmm(t):
                    ch.wait(t)
                    for k in range(8):
                        t.matmul(ps_sm[:, 0:128], kfm[:, k, :], kfm[:, k, :],
                                 start=(k == 0), stop=False)
                    inst = t.matmul(ps_sm[:, 0:128], ones_r[:, :],
                                    ones_r[:, :], start=False, stop=True)
                    ch.push(inst)
                bk.tensor(kkmm)

                def nev(v):
                    ch.wait(v)
                    ch.push(v.scalar_tensor_tensor(nmat[:, :], ps_sm[:, 0:128], -LAM,
                                                   mstr[:, :], ALU.mult, ALU.mult))
                bk.vector(nev)

                def tinit(v):
                    ch.wait(v)
                    t1 = ch.push(v.tensor_tensor(tmat[:, :], nmat[:, :], id_bf[:, :], ALU.add))
                    ch.wait(v, t1)
                    ch.push(v.tensor_copy(pmat[:, :], nmat[:, :]))
                bk.vector(tinit)

                for it in range(6):
                    def sq1(t, it=it):
                        ch.wait(t)
                        ch.push(t.transpose(ps_tr[0][:, 0:128], pmat[:, :], id_bf[:, :]))
                    bk.tensor(sq1)

                    def sq2(v, it=it):
                        ch.wait(v)
                        ch.push(v.tensor_copy(ptmat[:, :], ps_tr[0][:, 0:128]))
                    bk.vector(sq2)

                    def sq3(t, it=it):
                        ch.wait(t)
                        ch.push(t.matmul(ps_sm[:, 0:128], ptmat[:, :], pmat[:, :],
                                         start=True, stop=True))
                        ch.push(t.matmul(ps_sm[:, 128:256], ptmat[:, :], tmat[:, :],
                                         start=True, stop=True))
                    bk.tensor(sq3)

                    def sq4(v, it=it):
                        ch.wait(v)
                        t1 = ch.push(v.tensor_copy(pmat[:, :], ps_sm[:, 0:128]))
                        ch.wait(v, t1)
                        ch.push(v.tensor_tensor(tmat[:, :], tmat[:, :], ps_sm[:, 128:256], ALU.add))
                    bk.vector(sq4)

                def kqmm(t):
                    ch.wait(t)
                    for k in range(8):
                        t.matmul(ps_sm[:, 0:128], kfm[:, k, :], qfm[:, k, :],
                                 start=(k == 0), stop=False)
                    inst = t.matmul(ps_sm[:, 0:128], ones_r[:, :], ones_r[:, :],
                                    start=False, stop=True)
                    ch.push(inst)
                bk.tensor(kqmm)

                def uev(v):
                    ch.wait(v)
                    ch.push(v.tensor_tensor(umat[:, :], ps_sm[:, 0:128], mincl[:, :], ALU.mult))
                bk.vector(uev)

                def gmm(t):
                    ch.wait(t)
                    for hh in range(2):
                        inst = t.matmul(ps_big[:, hh * 512:(hh + 1) * 512], tmat[:, :],
                                        base_sb[:, hh * 512:(hh + 1) * 512], start=True, stop=True)
                    ch.push(inst)
                bk.tensor(gmm)

                def gev(s):
                    ch.wait(s)
                    ch.push(s.activation(gs_tm[:, :], ps_big[:, :], AF.Identity, scale=-LR))
                bk.scalar(gev)

                def omm(t, b=b):
                    ch.wait(t)
                    for k in range(8):
                        for hh in range(2):
                            t.matmul(ps_big[:, hh * 512:(hh + 1) * 512], qfm[:, k, :],
                                     wt[b][:, k, hh * 512:(hh + 1) * 512],
                                     start=(k == 0), stop=False)
                    for hh in range(2):
                        t.matmul(ps_big[:, hh * 512:(hh + 1) * 512], ones_r[:, :],
                                 bst_bf[0:1, hh * 512:(hh + 1) * 512], start=False, stop=False)
                    for hh in range(2):
                        inst = t.matmul(ps_big[:, hh * 512:(hh + 1) * 512], umat[:, :],
                                        gs_tm[:, hh * 512:(hh + 1) * 512], start=False, stop=True)
                    ch.push(inst)
                bk.tensor(omm)

                def dmx(sy, b=b, t0=t0):
                    ch.wait(sy)
                    outs.wait(sy)
                    dch.push(sy.dma_start(xtile[:, :], x_in[b, t0:t0 + CK, :]), 16)
                bk.sync(dmx)

                def oev(v):
                    ch.wait(v)
                    dch.wait(v)
                    ch.push(v.tensor_tensor(o_tm[:, :], ps_big[:, :], xtile[:, :], ALU.add))
                bk.vector(oev)

                def dmo(sy, b=b, t0=t0):
                    ch.wait(sy)
                    outs.push(sy.dma_start(x_out[b, t0:t0 + CK, :], o_tm[:, :]), 16)
                bk.sync(dmo)

                pe0, ev0 = pe_s.n, ev_s.n

                def ktr(t, pe0=pe0, ev0=ev0):
                    ch.wait(t)
                    for i in range(8):
                        if i >= 2:
                            ev_s.wait(t, ev0 + (i - 2) + 1)
                        pe_s.push(t.transpose(ps_tr[i % 2][:, 0:128], kfm[:, i, :], id_bf[:, :]))
                bk.tensor(ktr)

                def kev(v, pe0=pe0, ev0=ev0):
                    for i in range(8):
                        pe_s.wait(v, pe0 + 1 + i)
                        ev_s.push(v.tensor_copy(ktm[:, i, :], ps_tr[i % 2][:, 0:128]))
                bk.vector(kev)

                def kjoin(v):
                    ev_s.wait(v)
                    ch.push(v.tensor_copy(lnm[:, :], lnm[:, :]))
                bk.vector(kjoin)

                for i in range(8):
                    def dwmm(t, i=i):
                        ch.wait(t)
                        for hh in range(2):
                            inst = t.matmul(ps_big[:, hh * 512:(hh + 1) * 512], ktm[:, i, :],
                                            gs_tm[:, hh * 512:(hh + 1) * 512], start=True, stop=True)
                        ch.push(inst)
                    bk.tensor(dwmm)

                    def dwup(v, b=b, i=i):
                        ch.wait(v)
                        ch.push(v.tensor_tensor(wt[b][:, i, :], ps_big[:, :], wt[b][:, i, :], ALU.add))
                    bk.vector(dwup)

                for hh in range(2):
                    def bmm(t, hh=hh):
                        ch.wait(t)
                        ch.push(t.matmul(ps_sm[0:1, :], ones_c[:, :],
                                         gs_tm[:, hh * 512:(hh + 1) * 512], start=True, stop=True))
                    bk.tensor(bmm)

                    def bup(v, hh=hh):
                        ch.wait(v)
                        ch.push(v.tensor_tensor(bst[0:1, hh * 512:(hh + 1) * 512], ps_sm[0:1, :],
                                                bst[0:1, hh * 512:(hh + 1) * 512], ALU.add))
                    bk.vector(bup)

                def bbf(v):
                    ch.wait(v)
                    ch.push(v.tensor_copy(bst_bf[:, :], bst[:, :]))
                bk.vector(bbf)

                def fin(sy):
                    outs.wait(sy)
                bk.sync(fin)

    return nc


def run_ttt(x, inputs):
    nc = _get("ttt", build_ttt)
    ident = np.eye(128, dtype=np.float32)
    ii, jj = np.meshgrid(np.arange(128), np.arange(128), indexing="ij")
    Mstrict = (ii < jj).astype(np.float32)
    Mincl = (ii <= jj).astype(np.float32)
    base = {
        "thK_T": np.ascontiguousarray(np.asarray(inputs["L1_attn_theta_K"]).T),
        "thV_T": np.ascontiguousarray(np.asarray(inputs["L1_attn_theta_V"]).T),
        "thQ_T": np.ascontiguousarray(np.asarray(inputs["L1_attn_theta_Q"]).T),
        "W0_T": np.ascontiguousarray(np.asarray(inputs["L1_attn_W0"]).T),
        "b0": np.ascontiguousarray(np.asarray(inputs["L1_attn_b0"]).reshape(1, DM)),
        "Mstrict": Mstrict,
        "Mincl": Mincl,
        "ident": ident,
    }
    in_maps = []
    for cc in range(NCORE):
        m = dict(base)
        m["x_in"] = np.ascontiguousarray(x[2 * cc:2 * cc + 2])
        in_maps.append(m)
    res = run_bass_kernel_spmd(nc, in_maps, core_ids=list(range(NCORE)))
    return np.concatenate([r["x_out"] for r in res.results], axis=0)




def build_lstm():
    nc = bass.Bass("TRN2", target_bir_lowering=False, debug=False)
    NB = 16
    TS = 256
    x_in = nc.dram_tensor("x_in", [NB, TS, DM], F32, kind="ExternalInput").ap()
    aw_d = nc.dram_tensor("aw", [6, DM, DM], F32, kind="ExternalInput").ap()
    ow2_d = nc.dram_tensor("ow2", [2, DM, DM], F32, kind="ExternalInput").ap()
    gb_d = nc.dram_tensor("gb", [128, 3, 8], F32, kind="ExternalInput").ap()
    bo_d = nc.dram_tensor("bo", [128, 8], F32, kind="ExternalInput").ap()
    ic_d = nc.dram_tensor("in_c", [128, 8], F32, kind="ExternalInput").ap()
    id_d = nc.dram_tensor("ident", [128, 128], F32, kind="ExternalInput").ap()
    x_out = nc.dram_tensor("x_out", [NB, TS, DM], F32, kind="ExternalOutput").ap()

    AW = nc.alloc_sbuf_tensor("l_AW", [128, 6, 8, DM], BF16)
    wv = [AW[:, i, :, :] for i in range(6)]
    xn3 = nc.alloc_sbuf_tensor("l_xn3", [128, 8, NB * TS], BF16)
    c_fm = nc.alloc_sbuf_tensor("l_c", [128, 8, TS], F32)
    c_bf = [nc.alloc_sbuf_tensor(f"l_cbf{i}", [128, 8, TS], BF16) for i in range(2)]
    f_sb = nc.alloc_sbuf_tensor("l_f", [128, TS], BF16)
    i_sb = nc.alloc_sbuf_tensor("l_i", [128, TS], BF16)
    g_sb = nc.alloc_sbuf_tensor("l_g", [128, TS], BF16)
    tt1 = nc.alloc_sbuf_tensor("l_tt1", [128, TS], F32)
    tt2 = nc.alloc_sbuf_tensor("l_tt2", [128, TS], F32)
    gb_sb = nc.alloc_sbuf_tensor("l_gb", [128, 3, 8], F32)
    bo_sb = nc.alloc_sbuf_tensor("l_bo", [128, 8], F32)
    ic_sb = nc.alloc_sbuf_tensor("l_ic", [128, 8], F32)
    onef = nc.alloc_sbuf_tensor("l_onef", [128, TS], F32)
    id_bf = nc.alloc_sbuf_tensor("l_id", [128, 128], BF16)
    eps_col = nc.alloc_sbuf_tensor("l_eps", [128, 1], F32)
    h_fm = nc.alloc_sbuf_tensor("l_hfm", [128, 8, TS], BF16)
    xtile = nc.alloc_sbuf_tensor("l_xtile", [128, DM], F32)
    xn_tm = nc.alloc_sbuf_tensor("l_xntm", [128, DM], BF16)
    lnscr = nc.alloc_sbuf_tensor("l_lnscr", [128, DM], BF16)
    lnm = nc.alloc_sbuf_tensor("l_lnm", [128, 1], F32)
    lns = nc.alloc_sbuf_tensor("l_lns", [128, 1], F32)
    lnr = nc.alloc_sbuf_tensor("l_lnr", [128, 1], F32)
    o_tm = nc.alloc_sbuf_tensor("l_otm", [128, DM], F32)

    ps_f = nc.alloc_psum_tensor("l_psf", [128, 512], F32)
    ps_i = nc.alloc_psum_tensor("l_psi", [128, 512], F32)
    ps_g = nc.alloc_psum_tensor("l_psg", [128, 512], F32)
    ps_tr = [nc.alloc_psum_tensor(f"l_pstr{i}", [128, 1024], BF16) for i in range(2)]

    ch = Chain(nc, "l_ch")
    pe_s = Chain(nc, "l_pe")
    ev_s = Chain(nc, "l_ev")
    dch = Chain(nc, "l_dch")
    wdma = Chain(nc, "l_wdma")
    outs = Chain(nc, "l_outs")

    with nc.Block() as bk:
        def gsec(g):
            wdma.push(g.dma_start(AW[:, :, :, :],
                                  aw_d.rearrange("w (k p) m -> p w k m", p=128)), 16)
            dch.push(g.dma_start(gb_sb[:, :, :], gb_d), 16)
            dch.push(g.dma_start(bo_sb[:, :], bo_d), 16)
            dch.push(g.dma_start(ic_sb[:, :], ic_d), 16)
            dch.push(g.dma_start(id_bf[:, :], id_d), 16)
            g.memset(eps_col[:, :], EPS)
            g.memset(onef[:, :], 1.0)
        bk.gpsimd(gsec)

        def cinit(s):
            dch.wait(s)
            for m in range(8):
                ch.push(s.activation(c_fm[:, m, :], onef[:, :], AF.Identity,
                                     scale=ic_sb[:, m:m + 1]))
        bk.scalar(cinit)

        def cbf0(v):
            ch.wait(v)
            ch.push(v.tensor_copy(c_bf[0][:, :, :], c_fm[:, :, :]))
        bk.vector(cbf0)

    for b in range(16):
        for tt in range(2):
            with nc.Block() as bk:
                def dmain(sy, b=b, tt=tt):
                    ch.wait(sy)
                    dch.push(sy.dma_start(xtile[:, :], x_in[b, tt * 128:(tt + 1) * 128, :]), 16)
                bk.sync(dmain)

                def vstart(v):
                    dch.wait(v)
                    ch.push(v.tensor_copy(lnm[:, :], eps_col[:, :]))
                bk.vector(vstart)

                _ln_emit(nc, bk, ch, xtile, xn_tm, lnscr, lnm, lns, lnr, eps_col)

                pe0, ev0 = pe_s.n, ev_s.n

                def trans(t, pe0=pe0, ev0=ev0):
                    ch.wait(t)
                    for j in range(8):
                        if j >= 2:
                            ev_s.wait(t, ev0 + (j - 2) + 1)
                        pe_s.push(t.transpose(ps_tr[j % 2][:, 0:128],
                                              xn_tm[:, j * 128:(j + 1) * 128], id_bf[:, :]))
                bk.tensor(trans)

                def evac(v, b=b, tt=tt, pe0=pe0, ev0=ev0):
                    col = b * TS + tt * 128
                    for j in range(8):
                        pe_s.wait(v, pe0 + 1 + j)
                        ev_s.push(v.tensor_copy(xn3[:, j, col:col + 128], ps_tr[j % 2][:, 0:128]))
                bk.vector(evac)

                def joinv(v):
                    ev_s.wait(v)
                    ch.push(v.tensor_copy(lnm[:, :], lnm[:, :]))
                bk.vector(joinv)

    for b in range(16):
        col = b * TS
        with nc.Block() as bk:
            for m in range(8):
                def gmm(t, b=b, m=m, col=col):
                    wdma.wait(t)
                    ch.wait(t)
                    for (ps, wi_, ui_) in [(ps_f, 0, 1), (ps_i, 2, 3), (ps_g, 4, 5)]:
                        for k in range(8):
                            t.matmul(ps[:, 0:TS], wv[wi_][:, k, m * 128:(m + 1) * 128],
                                     xn3[:, k, col:col + TS], start=(k == 0), stop=False)
                        for k in range(8):
                            inst = t.matmul(ps[:, 0:TS], wv[ui_][:, k, m * 128:(m + 1) * 128],
                                            c_bf[b % 2][:, k, :], start=False, stop=(k == 7))
                    ch.push(inst)
                bk.tensor(gmm)

                def gact(s, m=m):
                    ch.wait(s)
                    t1 = ch.push(s.activation(f_sb[:, :], ps_f[:, 0:TS], AF.Sigmoid,
                                              bias=gb_sb[:, 0, m:m + 1]))
                    t2 = ch.push(s.activation(i_sb[:, :], ps_i[:, 0:TS], AF.Sigmoid,
                                              bias=gb_sb[:, 1, m:m + 1]))
                    ch.push(s.activation(g_sb[:, :], ps_g[:, 0:TS], AF.Tanh,
                                         bias=gb_sb[:, 2, m:m + 1]))
                bk.scalar(gact)

                def cup(v, b=b, m=m):
                    ch.wait(v)
                    t1 = ch.push(v.tensor_tensor(tt1[:, :], f_sb[:, :], c_fm[:, m, :], ALU.mult))
                    t2 = ch.push(v.tensor_tensor(tt2[:, :], i_sb[:, :], g_sb[:, :], ALU.mult))
                    ch.wait(v, t2)
                    t3 = ch.push(v.tensor_tensor(c_fm[:, m, :], tt1[:, :], tt2[:, :], ALU.add))
                    ch.wait(v, t3)
                    ch.push(v.tensor_copy(c_bf[(b + 1) % 2][:, m, :], c_fm[:, m, :]))
                bk.vector(cup)

    with nc.Block() as bk:
        def wload2(g):
            wdma.push(g.dma_start(AW[:, 0:2, :, :],
                                  ow2_d.rearrange("w (k p) m -> p w k m", p=128)), 16)
        bk.gpsimd(wload2)

    for b in range(16):
        col = b * TS
        with nc.Block() as bk:
            for m in range(8):
                def hmm(t, b=b, m=m, col=col):
                    wdma.wait(t)
                    ch.wait(t)
                    for k in range(8):
                        t.matmul(ps_f[:, 0:TS], wv[0][:, k, m * 128:(m + 1) * 128],
                                 xn3[:, k, col:col + TS], start=(k == 0), stop=False)
                    for k in range(8):
                        inst = t.matmul(ps_f[:, 0:TS], wv[1][:, k, m * 128:(m + 1) * 128],
                                        c_bf[0][:, k, :], start=False, stop=(k == 7))
                    ch.push(inst)
                bk.tensor(hmm)

                def hact(s, m=m):
                    ch.wait(s)
                    ch.push(s.activation(h_fm[:, m, :], ps_f[:, 0:TS], AF.Tanh,
                                         bias=bo_sb[:, m:m + 1]))
                bk.scalar(hact)

            for tt in range(2):
                def dmx(sy, b=b, tt=tt):
                    ch.wait(sy)
                    outs.wait(sy)
                    dch.push(sy.dma_start(xtile[:, :], x_in[b, tt * 128:(tt + 1) * 128, :]), 16)
                bk.sync(dmx)

                pe0, ev0 = pe_s.n, ev_s.n

                def etr(t, tt=tt, pe0=pe0, ev0=ev0):
                    ch.wait(t)
                    for m in range(8):
                        if m >= 2:
                            ev_s.wait(t, ev0 + (m - 2) + 1)
                        pe_s.push(t.transpose(ps_tr[m % 2][:, 0:128],
                                              h_fm[:, m, tt * 128:(tt + 1) * 128], id_bf[:, :]))
                bk.tensor(etr)

                def eev(v, tt=tt, pe0=pe0, ev0=ev0):
                    dch.wait(v)
                    for m in range(8):
                        pe_s.wait(v, pe0 + 1 + m)
                        ev_s.push(v.tensor_tensor(o_tm[:, m * 128:(m + 1) * 128],
                                                  ps_tr[m % 2][:, 0:128],
                                                  xtile[:, m * 128:(m + 1) * 128], ALU.add))
                bk.vector(eev)

                def ejoin(v):
                    ev_s.wait(v)
                    ch.push(v.tensor_copy(lnm[:, :], lnm[:, :]))
                bk.vector(ejoin)

                def dmo(sy, b=b, tt=tt):
                    ch.wait(sy)
                    outs.push(sy.dma_start(x_out[b, tt * 128:(tt + 1) * 128, :], o_tm[:, :]), 16)
                bk.sync(dmo)

            def fin(sy):
                outs.wait(sy)
            bk.sync(fin)

    return nc


def run_lstm(x, inputs):
    nc = _get("lstm", build_lstm)
    ident = np.eye(128, dtype=np.float32)
    aw = np.stack([np.asarray(inputs["L1_ffn_" + n]).T for n in
                   ["wf", "uf", "wi", "ui", "wc", "uc"]])
    ow2 = np.stack([np.asarray(inputs["L1_ffn_wo"]).T, np.asarray(inputs["L1_ffn_uo"]).T])
    gb = np.stack([np.asarray(inputs["L1_ffn_" + n]).reshape(8, 128).T for n in
                   ["bf", "bi", "bc"]], axis=1)
    base = {
        "aw": np.ascontiguousarray(aw),
        "ow2": np.ascontiguousarray(ow2),
        "gb": np.ascontiguousarray(gb),
        "bo": np.ascontiguousarray(np.asarray(inputs["L1_ffn_bo"]).reshape(8, 128).T),
        "in_c": np.ascontiguousarray(np.asarray(inputs["L1_ffn_in_c"]).reshape(8, 128).T),
        "ident": ident,
    }
    in_maps = []
    for cc in range(NCORE):
        m = dict(base)
        m["x_in"] = np.ascontiguousarray(x[:, 256 * cc:256 * (cc + 1), :])
        in_maps.append(m)
    res = run_bass_kernel_spmd(nc, in_maps, core_ids=list(range(NCORE)))
    return np.concatenate([r["x_out"] for r in res.results], axis=1)
